# version 18
# speedup vs baseline: 1.1949x; 1.0061x over previous
import math
import numpy as np
import ml_dtypes
import concourse.bass as bass
import concourse.mybir as mybir
from concourse.bass_utils import run_bass_kernel_spmd

F32 = mybir.dt.float32
F16 = mybir.dt.float16
BF16 = mybir.dt.bfloat16
AF = mybir.ActivationFunctionType
ALU = mybir.AluOpType
AX = mybir.AxisListType

NPF16 = np.float16
NPBF16 = ml_dtypes.bfloat16

D = 1024
T = 2048
NCH = 8
CW = T + 1
NSEQ = 2
NH_OUT = 16
NH_IN = 8
HID = 2730
NOC1 = 22
C_IN = 49.0

PHASE_SET = None
TAPS = {}


def cdiv(a, b):
    return (a + b - 1) // b


class Sem:
    def __init__(self, nc, name):
        self._cm = nc.semaphore(name)
        self.h = self._cm.__enter__()
        self.n = 0

    def inc(self, instr, k=1):
        self.n += k
        return instr.then_inc(self.h, k)


class Build:

    def __init__(self, nc):
        self.nc = nc
        self.phase_idx = 0
        self._stack = []
        self.enabled = []
        self._uid = 0

    def sb(self, name, shape, dtype):
        self._uid += 1
        return self.nc.sbuf_tensor(f"{name}_{self._uid}", shape, dtype)

    def ph_on(self, name):
        return PHASE_SET is None or name in PHASE_SET

    def declare_io(self):
        nc = self.nc
        dt = nc.dram_tensor
        self.xin = dt("xin", [NSEQ, T, D], F32, kind="ExternalInput").ap()
        self.out = dt("out", [NSEQ, T, D], F32, kind="ExternalOutput").ap()
        self.wqkvT = dt("wqkvT", [D, 3 * D], F32, kind="ExternalInput").ap()
        self.woutT = dt("woutT", [D, D], F32, kind="ExternalInput").ap()
        self.wqkv_i = dt("wqkv_i", [2, D, 1536], F32, kind="ExternalInput").ap()
        self.wo_i = dt("wo_i", [2, 512, D], F32, kind="ExternalInput").ap()
        self.w1_i = dt("w1_i", [2, D, 5460], F32, kind="ExternalInput").ap()
        self.w2_i = dt("w2_i", [2, HID, D], F32, kind="ExternalInput").ap()
        self.idf32 = dt("idf32", [128, 128], F32, kind="ExternalInput").ap()
        self.monesb = dt("monesb", [128, 128], BF16, kind="ExternalInput").ap()
        self.ones1h = dt("ones1h", [128, 1], F16, kind="ExternalInput").ap()
        self.ones1b = dt("ones1b", [128, 1], BF16, kind="ExternalInput").ap()
        self.ones2b = dt("ones2b", [128, 2], BF16, kind="ExternalInput").ap()
        self.one1f = dt("one1f", [1, 128], F32, kind="ExternalInput").ap()
        self.one1b = dt("one1b", [1, 128], BF16, kind="ExternalInput").ap()
        self.bc2f = dt("bc2f", [2, 128], F32, kind="ExternalInput").ap()
        self.lng = dt("lng", [128, 16], F32, kind="ExternalInput").ap()
        self.lnb = dt("lnb", [128, 16], F32, kind="ExternalInput").ap()
        self.b2c = dt("b2c", [128, 16], F32, kind="ExternalInput").ap()
        self.b1a = dt("b1a", [128, 2 * NOC1], F32, kind="ExternalInput").ap()
        self.b1g = dt("b1g", [128, 2 * NOC1], F32, kind="ExternalInput").ap()
        self.qscc = dt("qscc", [128, 2], F32, kind="ExternalInput").ap()
        self.kscc = dt("kscc", [128, 2], F32, kind="ExternalInput").ap()
        self.m49 = dt("m49", [128, 1], F32, kind="ExternalInput").ap()
        self.mtril = dt("mtril", [128, 128], F16, kind="ExternalInput").ap()
        self.mtriu = dt("mtriu", [128, 128], F16, kind="ExternalInput").ap()
        self.idf16 = dt("idf16", [128, 128], F16, kind="ExternalInput").ap()
        self.e0m = dt("e0m", [128, 128], F16, kind="ExternalInput").ap()
        self.e64m = dt("e64m", [128, 128], F16, kind="ExternalInput").ap()
        self.rotq = dt("rotq", [128, 2 * T], F32, kind="ExternalInput").ap()
        self.rotk = dt("rotk", [128, 2 * T], F32, kind="ExternalInput").ap()
        self.x1st = dt("x1stash", [128, 8 * T], F16, kind="Internal").ap()
        self.taps = {}
        for name in TAPS:
            self.taps[name] = dt("tap_" + name, [NSEQ, 128, NCH * CW], F32,
                                 kind="ExternalOutput").ap()

    def alloc_persistent(self, stack):
        nc = self.nc
        en = stack.enter_context
        self.X = en(nc.sbuf_tensor("X", [128, NCH * CW], F32))
        self.XC = en(nc.sbuf_tensor("XC", [128, NCH * CW], F16))
        self.cidf32 = en(nc.sbuf_tensor("cidf32", [128, 128], F32))
        self.cmones = en(nc.sbuf_tensor("cmones", [128, 128], BF16))
        self.cones1h = en(nc.sbuf_tensor("cones1h", [128, 1], F16))
        self.cones1b = en(nc.sbuf_tensor("cones1b", [128, 1], BF16))
        self.cones2b = en(nc.sbuf_tensor("cones2b", [128, 2], BF16))
        self.cone1f = en(nc.sbuf_tensor("cone1f", [1, 128], F32))
        self.cone1b = en(nc.sbuf_tensor("cone1b", [1, 128], BF16))
        self.cbc2f = en(nc.sbuf_tensor("cbc2f", [2, 128], F32))
        self.clng = en(nc.sbuf_tensor("clng", [128, 16], F32))
        self.clnb = en(nc.sbuf_tensor("clnb", [128, 16], F32))
        self.cb2 = en(nc.sbuf_tensor("cb2", [128, 16], F32))
        self.cb1a = en(nc.sbuf_tensor("cb1a", [128, 2 * NOC1], F32))
        self.cb1g = en(nc.sbuf_tensor("cb1g", [128, 2 * NOC1], F32))
        self.cqsc = en(nc.sbuf_tensor("cqsc", [128, 2], F32))
        self.cksc = en(nc.sbuf_tensor("cksc", [128, 2], F32))
        self.MU = en(nc.sbuf_tensor("MU", [1, 512], F32))
        self.ISD = en(nc.sbuf_tensor("ISD", [1, 512], F32))
        self.KMX = en(nc.sbuf_tensor("KMX", [2, 8], F32))
        self.KM2 = en(nc.sbuf_tensor("KM2", [128, 8], BF16))
        self.cm49 = en(nc.sbuf_tensor("cm49", [128, 1], F32))
        self.cmtril = en(nc.sbuf_tensor("cmtril", [128, 128], F16))
        self.cmtriu = en(nc.sbuf_tensor("cmtriu", [128, 128], F16))
        self.cidf16 = en(nc.sbuf_tensor("cidf16", [128, 128], F16))
        self.cE0 = en(nc.sbuf_tensor("cE0", [128, 128], F16))
        self.cE64 = en(nc.sbuf_tensor("cE64", [128, 128], F16))
        nc.const_aps.register(F32, -C_IN, self.cm49.ap()[:, 0:1])
        self.PSA = en(nc.psum_tensor("PSA", [128, 2048], F32))
        self.PSB = en(nc.psum_tensor("PSB", [128, 2048], F32))
        self.BAR = Sem(nc, "BAR"); stack.push(self.BAR._cm)
        names = ["dma_x", "dma_w", "dma_o", "te", "sc", "ve", "gp",
                 "te2", "sc2", "ve2", "wfree", "dxa", "dxb", "dwa", "dwb",
                 "dma_s"]
        for n in names:
            s = Sem(nc, n)
            stack.push(s._cm)
            setattr(self, n, s)

    def Xv(self, c, c0, c1):
        return self.X.ap()[:, c * CW + c0: c * CW + c1]

    def Xtok(self, c, t0, t1):
        return self.X.ap()[:, c * CW + 1 + t0: c * CW + 1 + t1]

    def XCtok(self, c, t0, t1):
        return self.XC.ap()[:, c * CW + 1 + t0: c * CW + 1 + t1]

    def XCshift(self, c, t0, t1):
        return self.XC.ap()[:, c * CW + t0: c * CW + t1]

    def XCin(self, c, t0, t1):
        if c < 4:
            return self.XCtok(c, t0, t1)
        return self.XCshift(c, t0, t1)

    def Xin(self, c, t0, t1):
        if c < 4:
            return self.Xtok(c, t0, t1)
        return self.X.ap()[:, c * CW + t0: c * CW + t1]

    def barrier(self):
        nc = self.nc
        self.phase_idx += 1
        target = 5 * self.phase_idx
        block = self.block
        for eng_name in ("tensor", "vector", "scalar", "gpsimd", "sync"):
            def body(eng, target=target):
                self.BAR.n += 1
                eng.drain()
                eng.nop().then_inc(self.BAR.h, 1)
                eng.wait_ge(self.BAR.h, target)
            getattr(block, eng_name)(body)
        assert self.BAR.n == target

    def p0(self, s):
        nc = self.nc
        blk = self.block
        with self.sb("p0tm", [128, 2 * D], F32) as xtm:
            dp = [self.dxa, self.dxb]
            d0 = [dp[0].n, dp[1].n]
            t0 = self.te.n
            v0 = self.ve.n

            def sync_body(sy):
                for tc in range(16):
                    if tc >= 2:
                        sy.wait_ge(self.te.h, t0 + 8 * (tc - 1))
                    dp[tc % 2].inc(
                        sy.dma_start(xtm.ap()[:, (tc % 2) * D:(tc % 2 + 1) * D],
                                     self.xin[s, tc * 128:(tc + 1) * 128, :]), 16)
            blk.sync(sync_body)

            def te_body(te):
                for tc in range(16):
                    te.wait_ge(dp[tc % 2].h, d0[tc % 2] + 16 * (tc // 2 + 1))
                    for c in range(8):
                        grp = (tc % 2) * 1024 + (c // 4) * 512
                        if tc >= 2 and c % 4 == 0:
                            te.wait_ge(self.ve.h, v0 + 2 * (tc - 2) + (c // 4) + 1)
                        self.te.inc(te.transpose(
                            self.PSA.ap()[:, grp + (c % 4) * 128: grp + (c % 4) * 128 + 128],
                            xtm.ap()[:, (tc % 2) * D + c * 128:(tc % 2) * D + (c + 1) * 128],
                            self.cidf32.ap()[:, :]), 1)
            blk.tensor(te_body)

            def ve_body(ve):
                xz = self.X.ap().rearrange("p (c w) -> p c w", w=CW)
                ve.memset(xz[:, :, 0:1], 0.0)
                for tc in range(16):
                    for half in range(2):
                        ve.wait_ge(self.te.h, t0 + 8 * tc + 4 * (half + 1))
                        grp = (tc % 2) * 1024 + half * 512
                        dst = self.X.ap().rearrange("p (c w) -> p c w", w=CW)[
                            :, half * 4:(half + 1) * 4, 1 + tc * 128:1 + (tc + 1) * 128]
                        self.ve.inc(ve.tensor_copy(
                            dst, self.PSA.ap()[:, grp:grp + 512]), 1)
            blk.vector(ve_body)
        self.barrier()

    def p6(self, s, full=True):
        nc = self.nc
        blk = self.block
        if full:
            self.emit_resid_xh(s)
            self.emit_outer_ln(s, make_xc=True, post_copies=False)
        with self.sb("p6tm", [128, 2 * D], F32) as otm:
            dp = [self.dxa, self.dxb]
            d0 = [dp[0].n, dp[1].n]
            t0 = self.te.n
            v0 = self.ve.n

            def te_body(te):
                for tc in range(16):
                    if tc >= 2:
                        te.wait_ge(self.ve.h, v0 + 2 * (tc - 1))
                    for c in range(8):
                        grp = (tc % 2) * 1024 + (c // 4) * 512
                        self.te.inc(te.transpose(
                            self.PSA.ap()[:, grp + (c % 4) * 128: grp + (c % 4) * 128 + 128],
                            self.Xtok(c, tc * 128, (tc + 1) * 128),
                            self.cidf32.ap()[:, :]), 1)
            blk.tensor(te_body)

            def ve_body(ve):
                for tc in range(16):
                    for half in range(2):
                        ve.wait_ge(self.te.h, t0 + 8 * tc + 4 * (half + 1))
                        if tc >= 2 and half == 0:
                            ve.wait_ge(dp[tc % 2].h, d0[tc % 2] + 16 * (tc // 2))
                        grp = (tc % 2) * 1024 + half * 512
                        dst = otm.ap()[:, (tc % 2) * D + half * 512:(tc % 2) * D + (half + 1) * 512]
                        self.ve.inc(ve.tensor_copy(
                            dst.rearrange("p (c n) -> p c n", n=128),
                            self.PSA.ap()[:, grp:grp + 512].rearrange("p (c n) -> p c n", n=128)), 1)
            blk.vector(ve_body)

            def sync_body(sy):
                for tc in range(16):
                    sy.wait_ge(self.ve.h, v0 + 2 * (tc + 1))
                    dp[tc % 2].inc(sy.dma_start(
                        self.out[s, tc * 128:(tc + 1) * 128, :],
                        otm.ap()[:, (tc % 2) * D:(tc % 2 + 1) * D]), 16)
                sy.wait_ge(dp[0].h, dp[0].n)
                sy.wait_ge(dp[1].h, dp[1].n)
            blk.sync(sync_body)
        self.barrier()

    def emit_x1_stash(self, s):
        blk = self.block

        def sync_body(sy):
            src = self.XC.ap().rearrange("p (c w) -> p c w", w=CW)[:, :, 1:CW]
            self.dma_s.inc(sy.dma_start(
                self.x1st.rearrange("p (c n) -> p c n", n=T), src), 16)
        blk.sync(sync_body)

    def emit_resid_xh(self, s):
        blk = self.block
        with self.sb("XHS", [128, 2 * T], F16) as XHS:
            dp = [self.dxa, self.dxb]
            d0 = [self.dxa.n, self.dxb.n]
            v0 = self.ve.n

            def sync_body(sy):
                sy.wait_ge(self.dma_s.h, self.dma_s.n)
                for c in range(8):
                    if c >= 2:
                        sy.wait_ge(self.ve.h, v0 + 4 * (c - 1))
                    dp[c % 2].inc(sy.dma_start(
                        XHS.ap()[:, (c % 2) * T:(c % 2 + 1) * T],
                        self.x1st[:, c * T:(c + 1) * T]), 16)
            blk.sync(sync_body)

            def ve_body(ve):
                for c in range(8):
                    ve.wait_ge(dp[c % 2].h, d0[c % 2] + 16 * (c // 2 + 1))
                    for tw in range(4):
                        self.ve.inc(ve.tensor_add(
                            self.Xtok(c, tw * 512, (tw + 1) * 512),
                            self.Xtok(c, tw * 512, (tw + 1) * 512),
                            XHS.ap()[:, (c % 2) * T + tw * 512:(c % 2) * T + (tw + 1) * 512]), 1)
            blk.vector(ve_body)
        self.barrier()

    def emit_outer_ln(self, s, make_xc, post_copies):
        nc = self.nc
        blk = self.block
        with (
            self.sb("lnsq", [128, 8 * 512], BF16) as sqt,
            self.sb("MU2", [1, 2048], F32) as MU2,
            self.sb("ISD2", [1, 2048], F32) as ISD2,
        ):
            v0 = self.ve.n
            t0 = self.te.n
            sc0 = self.sc.n
            t2_0 = self.te2.n
            v2_0 = self.ve2.n
            napply = 3 if post_copies else 2

            def ve_main(ve):
                xz = self.XC.ap().rearrange("p (c w) -> p c w", w=CW)
                ve.memset(xz[:, :, 0:1], 0.0)
                for tw in range(4):
                    if tw >= 1:
                        ve.wait_ge(self.te.h, t0 + 16 * tw)
                    for c in range(8):
                        self.ve.inc(ve.tensor_copy(
                            self.XCtok(c, tw * 512, (tw + 1) * 512),
                            self.Xtok(c, tw * 512, (tw + 1) * 512)), 1)
                    ve.drain()
                    for c in range(8):
                        self.ve.inc(ve.tensor_mul(
                            sqt.ap()[:, c * 512:(c + 1) * 512],
                            self.XCtok(c, tw * 512, (tw + 1) * 512),
                            self.XCtok(c, tw * 512, (tw + 1) * 512)), 1)
                ve.wait_ge(self.te.h, t0 + 64)
                mu = MU2.ap()[0:1, 0:2048]
                isd = ISD2.ap()[0:1, 0:2048]
                self.ve.inc(ve.tensor_scalar(
                    mu, self.PSB.ap()[0:1, 0:2048], 1.0 / 1024.0, None, ALU.mult), 1)
                ve.drain()
                self.ve.inc(ve.tensor_mul(isd, mu, mu), 1)
                ve.drain()
                self.ve.inc(ve.scalar_tensor_tensor(
                    isd, self.PSA.ap()[0:1, 0:2048], 1.0 / 1024.0, isd,
                    ALU.mult, ALU.subtract), 1)
                ve.drain()
                self.ve.inc(ve.tensor_scalar(isd, isd, 1e-5, None, ALU.add), 1)
                ve.drain()
                self.ve.inc(ve.reciprocal(isd, isd), 1)
            blk.vector(ve_main)
            assert self.ve.n == v0 + 69

            def te_sums(te):
                for tw in range(4):
                    te.wait_ge(self.ve.h, v0 + 16 * tw + 8)
                    for c in range(8):
                        self.te.inc(te.matmul(
                            self.PSB.ap()[0:1, tw * 512:(tw + 1) * 512],
                            self.cones1h.ap()[:, :],
                            self.XCtok(c, tw * 512, (tw + 1) * 512),
                            start=(c == 0), stop=(c == 7)), 1)
                    te.wait_ge(self.ve.h, v0 + 16 * (tw + 1))
                    for c in range(8):
                        self.te.inc(te.matmul(
                            self.PSA.ap()[0:1, tw * 512:(tw + 1) * 512],
                            self.cones1b.ap()[:, :],
                            sqt.ap()[:, c * 512:(c + 1) * 512],
                            start=(c == 0), stop=(c == 7)), 1)
                te.wait_ge(self.sc.h, sc0 + 1)
                for tw in range(4):
                    self.te2.inc(te.matmul(
                        self.PSB.ap()[:, tw * 512:(tw + 1) * 512],
                        self.cone1f.ap()[:, :],
                        MU2.ap()[0:1, tw * 512:(tw + 1) * 512],
                        start=True, stop=True), 1)
                    self.te2.inc(te.matmul(
                        self.PSA.ap()[:, tw * 512:(tw + 1) * 512],
                        self.cone1f.ap()[:, :],
                        ISD2.ap()[0:1, tw * 512:(tw + 1) * 512],
                        start=True, stop=True), 1)
            blk.tensor(te_sums)
            assert self.te.n == t0 + 64 and self.te2.n == t2_0 + 8

            def sc_rows(sc):
                sc.wait_ge(self.ve.h, v0 + 69)
                self.sc.inc(sc.activation(
                    ISD2.ap()[0:1, 0:2048],
                    ISD2.ap()[0:1, 0:2048], AF.Sqrt), 1)
            blk.scalar(sc_rows)
            assert self.sc.n == sc0 + 1

            def ve_apply(ve):
                for tw in range(4):
                    ve.wait_ge(self.te2.h, t2_0 + 2 * (tw + 1))
                    mub = self.PSB.ap()[:, tw * 512:(tw + 1) * 512]
                    isb = self.PSA.ap()[:, tw * 512:(tw + 1) * 512]
                    for c in range(8):
                        xs = self.Xtok(c, tw * 512, (tw + 1) * 512)
                        self.ve2.inc(ve.tensor_sub(xs, xs, mub), 1)
                    ve.drain()
                    for c in range(8):
                        xs = self.Xtok(c, tw * 512, (tw + 1) * 512)
                        self.ve2.inc(ve.tensor_mul(xs, xs, isb), 1)
                    if post_copies:
                        ve.drain()
                        for c in range(8):
                            self.ve2.inc(ve.tensor_copy(
                                self.XCtok(c, tw * 512, (tw + 1) * 512),
                                self.Xtok(c, tw * 512, (tw + 1) * 512)), 1)
            blk.vector(ve_apply)
            assert self.ve2.n == v2_0 + 8 * napply * 4
        self.barrier()

    def p12(self, s):
        nc = self.nc
        blk = self.block
        with (
            self.sb("VO", [128, 16 * 1040], BF16) as VO,
            self.sb("QB", [128, 16 * 512], F16) as QB,
            self.sb("XCO", [128, 8 * 512], F16) as XCO,
            self.sb("PT", [128, 6 * 512], BF16) as PT,
            self.sb("QSQ", [128, 1024], BF16) as QSQ,
            self.sb("WS2", [128, 2 * 8 * 128], F16) as WS2,
            self.sb("RC", [1, 1024], BF16) as RC,
            self.sb("RBs", [128, 512], BF16) as RBs,
        ):
            self.RC, self.RBs = RC, RBs
            with self.sb("KO", [128, 8 * T], F16) as KO:
                self.p1_kv(s, KO, VO, WS2, QSQ, QB, XCO)
                self.p2_attn(s, KO, VO, QB, XCO, PT, QSQ, WS2)

    def p1_kv(self, s, KO, VO, WSK, KSQ, QB, XCO):
        nc = self.nc
        blk = self.block
        wp = [self.dwa, self.dwb]
        d0 = [wp[0].n, wp[1].n]
        t0 = self.te.n
        v0 = self.ve.n
        s0 = self.sc.n

        def ve_cast(ve):
            ve.wait_ge(self.dma_s.h, self.dma_s.n)
            vov = VO.ap().rearrange("p (tc h w) -> p tc h w", h=16, w=65)
            ve.memset(vov[:, :, :, 64:65], 1.0)
            ve.memset(self.KMX.ap()[:, :], 0.0)
            for b in range(8):
                ve.memset(QB.ap()[64:128, (2 * b) * 512:(2 * b) * 512 + 512], 0.0)
                ve.memset(QB.ap()[0:64, (2 * b + 1) * 512:(2 * b + 1) * 512 + 512], 0.0)
            ve.memset(XCO.ap()[:, :], 0.0)
            for c in range(8):
                for tw in range(4):
                    self.ve.inc(ve.tensor_copy(
                        self.XCtok(c, tw * 512, (tw + 1) * 512),
                        self.Xtok(c, tw * 512, (tw + 1) * 512)), 1)
        blk.vector(ve_cast)
        assert self.ve.n == v0 + 32

        def gp_k(gp):
            for oc in range(8):
                if oc >= 2:
                    gp.wait_ge(self.sc.h, s0 + 4 * (oc - 1))
                src = self.wqkvT[:, 1024 + oc * 128: 1024 + (oc + 1) * 128]
                src = src.rearrange("(ic p) n -> p ic n", p=128)
                dst = WSK.ap()[:, (oc % 2) * 1024:(oc % 2 + 1) * 1024]
                dst = dst.rearrange("p (ic n) -> p ic n", n=128)
                wp[oc % 2].inc(gp.dma_start(dst, src), 16)
        blk.gpsimd(gp_k)

        def te_k(te):
            te.wait_ge(self.ve.h, v0 + 32)
            for oc in range(8):
                te.wait_ge(wp[oc % 2].h, d0[oc % 2] + 16 * (oc // 2 + 1))
                ps = self.PSA if oc % 2 == 0 else self.PSB
                if oc >= 2:
                    te.wait_ge(self.sc.h, s0 + 4 * (oc - 1))
                for ic in range(8):
                    for tw in range(4):
                        ins = te.matmul(
                            ps.ap()[:, tw * 512:(tw + 1) * 512],
                            WSK.ap()[:, (oc % 2) * 1024 + ic * 128:(oc % 2) * 1024 + (ic + 1) * 128],
                            self.XCtok(ic, tw * 512, (tw + 1) * 512),
                            start=(ic == 0), stop=(ic == 7))
                        if ic == 7:
                            self.te.inc(ins, 1)
        blk.tensor(te_k)
        assert self.te.n == t0 + 32

        def sc_k(sc):
            for oc in range(8):
                ps = self.PSA if oc % 2 == 0 else self.PSB
                for tw in range(4):
                    sc.wait_ge(self.te.h, t0 + 4 * oc + tw + 1)
                    self.sc.inc(sc.copy(
                        KO.ap()[:, oc * T + tw * 512: oc * T + (tw + 1) * 512],
                        ps.ap()[:, tw * 512:(tw + 1) * 512]), 1)
        blk.scalar(sc_k)
        assert self.sc.n == s0 + 32

        WSV_h = None
        d1 = self.dma_w.n
        t3 = self.te.n
        s1 = self.sc.n

        def gp_v(gp):
            for vh in range(8):
                if vh == 0:
                    gp.wait_ge(self.te.h, t0 + 32)
                else:
                    gp.wait_ge(self.sc.h, s1 + 16 * vh)
                src = self.wqkvT[:, 2048 + vh * 128: 2048 + (vh + 1) * 128]
                src = src.rearrange("(ic p) n -> p ic n", p=128)
                dst = WSK.ap()[:, 0:1024].rearrange("p (ic n) -> p ic n", n=128)
                self.dma_w.inc(gp.dma_start(dst, src), 16)
        blk.gpsimd(gp_v)

        def te_v(te):
            te.wait_ge(self.ve.h, v0 + 32)
            te.wait_ge(self.sc.h, s0 + 32)
            for vh in range(8):
                te.wait_ge(self.dma_w.h, d1 + 16 * (vh + 1))
                for tc in range(16):
                    i = vh * 16 + tc
                    if i >= 4:
                        te.wait_ge(self.sc.h, s1 + i - 3)
                    for ic in range(8):
                        ins = te.matmul(
                            self.PSA.ap()[:, (i % 4) * 512:(i % 4) * 512 + 128],
                            self.XCtok(ic, tc * 128, (tc + 1) * 128),
                            WSK.ap()[:, ic * 128:(ic + 1) * 128],
                            start=(ic == 0), stop=(ic == 7))
                        if ic == 7:
                            self.te.inc(ins, 1)
        blk.tensor(te_v)
        assert self.te.n == t3 + 128

        def sc_v(sc):
            vov = VO.ap().rearrange("p (tc h w) -> p tc h w", h=16, w=65)
            for vh in range(8):
                for tc in range(16):
                    i = vh * 16 + tc
                    sc.wait_ge(self.te.h, t3 + i + 1)
                    dst = vov[:, tc, vh * 2:(vh + 1) * 2, 0:64]
                    src = self.PSA.ap()[:, (i % 4) * 512:(i % 4) * 512 + 128]
                    src = src.rearrange("p (h w) -> p h w", w=64)
                    self.sc.inc(sc.copy(dst, src), 1)
        blk.scalar(sc_v)
        assert self.sc.n == s1 + 128

        v1 = self.ve.n
        t1 = self.te.n

        def ve_knorm(ve):
            for oc in range(8):
                if oc >= 1:
                    ve.wait_ge(self.te.h, t1 + 4 * oc)
                    self.ve.inc(ve.reduce_max(
                        self.KMX.ap()[:, oc - 1:oc], self.PSB.ap()[0:2, 0:2048], AX.X), 1)
                for tw in range(4):
                    i = oc * 4 + tw
                    ve.wait_ge(self.sc.h, s0 + 4 * oc + tw + 1)
                    if i >= 1:
                        ve.wait_ge(self.te.h, t1 + i)
                    self.ve.inc(ve.tensor_mul(
                        KSQ.ap()[:, 0:512],
                        KO.ap()[:, oc * T + tw * 512: oc * T + (tw + 1) * 512],
                        KO.ap()[:, oc * T + tw * 512: oc * T + (tw + 1) * 512]), 1)
            ve.wait_ge(self.te.h, t1 + 32)
            self.ve.inc(ve.reduce_max(
                self.KMX.ap()[:, 7:8], self.PSB.ap()[0:2, 0:2048], AX.X), 1)
        blk.vector(ve_knorm)
        assert self.ve.n == v1 + 40

        def sq_pos(i):
            oc, tw = divmod(i, 4)
            return v1 + (tw + 1 if oc == 0 else 5 * oc + 1 + tw)

        def te_ksum(te):
            te.wait_ge(self.sc.h, s0 + 32)
            for oc in range(8):
                for tw in range(4):
                    i = oc * 4 + tw
                    te.wait_ge(self.ve.h, sq_pos(i))
                    if oc >= 1 and tw == 0:
                        te.wait_ge(self.ve.h, v1 + 5 * oc)
                    self.te.inc(te.matmul(
                        self.PSB.ap()[0:2, tw * 512:(tw + 1) * 512],
                        self.cones2b.ap()[:, :],
                        KSQ.ap()[:, 0:512],
                        start=True, stop=True), 1)
        blk.tensor(te_ksum)
        assert self.te.n == t1 + 32

        t2 = self.te2.n

        def te_km2(te):
            te.wait_ge(self.ve.h, v1 + 40)
            self.te2.inc(te.matmul(
                self.PSB.ap()[:, 0:8], self.cbc2f.ap()[:, :], self.KMX.ap()[:, :],
                start=True, stop=True), 1)
        blk.tensor(te_km2)


        def ve_km2(ve):
            ve.wait_ge(self.te2.h, t2 + 1)
            self.ve.inc(ve.tensor_copy(self.KM2.ap()[:, :], self.PSB.ap()[:, 0:8]), 1)
        blk.vector(ve_km2)

        self.barrier()

    def p2_attn(self, s, KO, VO, QB, XCO, PT, QSQ, WS2):
        nc = self.nc
        blk = self.block
        for qb in range(4):
            q0 = qb * 512
            wp = [self.dwa, self.dwb]
            d0 = [wp[0].n, wp[1].n]
            t0 = self.te.n
            s0 = self.sc.n
            v0 = self.ve.n
            s2q = self.sc2.n

            def gp_q(gp):
                for oc in range(8):
                    if oc < 2:
                        gp.wait_ge(self.ve.h, v0)
                    else:
                        gp.wait_ge(self.ve.h, v0 + 2 * (oc - 2) + 2)
                    src = self.wqkvT[:, oc * 128:(oc + 1) * 128]
                    src = src.rearrange("(ic p) n -> p ic n", p=128)
                    dst = WS2.ap()[:, (oc % 2) * 1024:(oc % 2 + 1) * 1024]
                    dst = dst.rearrange("p (ic n) -> p ic n", n=128)
                    wp[oc % 2].inc(gp.dma_start(dst, src), 16)
            blk.gpsimd(gp_q)

            def te_q(te):
                for oc in range(8):
                    te.wait_ge(wp[oc % 2].h, d0[oc % 2] + 16 * (oc // 2 + 1))
                    if oc == 0:
                        te.wait_ge(self.ve.h, v0)
                    if oc >= 2:
                        te.wait_ge(self.ve.h, v0 + 2 * (oc - 2) + 2)
                    for ic in range(8):
                        ins = te.matmul(
                            self.PSB.ap()[:, (oc % 2) * 512:(oc % 2) * 512 + 512],
                            WS2.ap()[:, (oc % 2) * 1024 + ic * 128:(oc % 2) * 1024 + (ic + 1) * 128],
                            self.XCtok(ic, q0, q0 + 512),
                            start=(ic == 0), stop=(ic == 7))
                        if ic == 7:
                            self.te.inc(ins, 1)
                for oc in range(8):
                    nb_ = 1024 + (oc % 2) * 512
                    te.wait_ge(self.ve.h, v0 + 16 + 2 * oc + 2)
                    if oc >= 2:
                        te.wait_ge(self.sc.h, s0 + oc - 1)
                    te.matmul(self.PSB.ap()[0:1, nb_:nb_ + 512],
                              self.KM2.ap()[0:64, oc:oc + 1],
                              QSQ.ap()[0:64, (oc % 2) * 512:(oc % 2) * 512 + 512],
                              start=True, stop=True, tile_position=(0, 0))
                    self.te.inc(te.matmul(
                        self.PSB.ap()[64:65, nb_:nb_ + 512],
                        self.KM2.ap()[64:128, oc:oc + 1],
                        QSQ.ap()[64:128, (oc % 2) * 512:(oc % 2) * 512 + 512],
                        start=True, stop=True, tile_position=(64, 64)), 1)
            blk.tensor(te_q)
            assert self.te.n == t0 + 16

            def ve_q(ve):
                for oc in range(8):
                    slot = self.PSB.ap()[:, (oc % 2) * 512:(oc % 2) * 512 + 512]
                    ve.wait_ge(self.te.h, t0 + oc + 1)
                    self.ve.inc(ve.tensor_copy(
                        QB.ap()[0:64, (2 * oc) * 512:(2 * oc) * 512 + 512],
                        slot[0:64, :]), 1)
                    self.ve.inc(ve.tensor_copy(
                        QB.ap()[64:128, (2 * oc + 1) * 512:(2 * oc + 1) * 512 + 512],
                        slot[64:128, :]), 1)
                ve.drain()
                for oc in range(8):
                    qa = QB.ap()[0:64, (2 * oc) * 512:(2 * oc) * 512 + 512]
                    qb_ = QB.ap()[64:128, (2 * oc + 1) * 512:(2 * oc + 1) * 512 + 512]
                    if oc >= 2:
                        ve.wait_ge(self.te.h, t0 + 8 + (oc - 2) + 1)
                    self.ve.inc(ve.tensor_mul(
                        QSQ.ap()[0:64, (oc % 2) * 512:(oc % 2) * 512 + 512],
                        qa, qa), 1)
                    self.ve.inc(ve.tensor_mul(
                        QSQ.ap()[64:128, (oc % 2) * 512:(oc % 2) * 512 + 512],
                        qb_, qb_), 1)
            blk.vector(ve_q)
            assert self.ve.n == v0 + 32

            def sc_q(sc):
                for oc in range(8):
                    nb_ = 1024 + (oc % 2) * 512
                    sc.wait_ge(self.te.h, t0 + 8 + oc + 1)
                    sc.activation(
                        XCO.ap()[0:1, oc * 512:(oc + 1) * 512],
                        self.PSB.ap()[0:1, nb_:nb_ + 512], AF.Sqrt)
                    self.sc.inc(sc.activation(
                        XCO.ap()[64:65, oc * 512:(oc + 1) * 512],
                        self.PSB.ap()[64:65, nb_:nb_ + 512], AF.Sqrt), 1)
            blk.scalar(sc_q)
            assert self.sc.n == s0 + 8

            def emit_av_pair(te, hp, kc, schp, vehp):
                pr = (hp % 2) * 1024
                te.wait_ge(self.sc.h, schp + kc + 1)
                if kc == 0:
                    te.wait_ge(self.ve.h, vehp - 6)
                self.te2.inc(te.matmul(
                    self.PSB.ap()[0:65, pr:pr + 512],
                    VO.ap()[:, kc * 1040 + (2 * hp) * 65: kc * 1040 + (2 * hp) * 65 + 65],
                    PT.ap()[:, (kc % 3) * 1024:(kc % 3) * 1024 + 512],
                    start=(kc == 0), stop=(kc == 15)), 1)
                self.te2.inc(te.matmul(
                    self.PSB.ap()[0:65, pr + 512:pr + 1024],
                    VO.ap()[:, kc * 1040 + (2 * hp + 1) * 65: kc * 1040 + (2 * hp + 1) * 65 + 65],
                    PT.ap()[:, (kc % 3) * 1024 + 512:(kc % 3) * 1024 + 1024],
                    start=(kc == 0), stop=(kc == 15)), 1)

            def emit_bc(te, hp, vehp):
                pr = (hp % 2) * 1024
                te.wait_ge(self.ve.h, vehp + 2)
                te.matmul(self.PSB.ap()[64:128, pr:pr + 512],
                          self.cone1b.ap()[0:1, 0:64],
                          self.RC.ap()[0:1, 0:512],
                          start=True, stop=True, tile_position=(0, 64))
                self.sc2.inc(te.matmul(
                    self.PSB.ap()[64:128, pr + 512:pr + 1024],
                    self.cone1b.ap()[0:1, 0:64],
                    self.RC.ap()[0:1, 512:1024],
                    start=True, stop=True, tile_position=(0, 64)), 1)

            pend = [None]
            for hp in range(8):
                tehp = self.te.n
                schp = self.sc.n
                avhp = self.te2.n
                vehp = self.ve.n

                def te_attn(te, hp=hp, tehp=tehp, schp=schp, vehp=vehp,
                            bc=pend[0]):
                    for kc in range(16):
                        sb_ = self.PSA.ap()[:, (kc % 2) * 1024:(kc % 2) * 1024 + 1024]
                        slotA = sb_[:, 0:512]
                        slotB = sb_[:, 512:1024]
                        if kc >= 2 or hp > 0:
                            te.wait_ge(self.sc.h, schp + kc - 1)
                        if kc == 6 and bc is not None:
                            emit_bc(te, *bc)
                        if hp == 0 and kc == 0:
                            te.wait_ge(self.ve.h, vehp)
                            te.wait_ge(self.sc.h, schp)
                        ktile = KO.ap()[:, hp * T + kc * 128: hp * T + (kc + 1) * 128]
                        te.matmul(slotA, ktile,
                                  QB.ap()[:, (2 * hp) * 512:(2 * hp) * 512 + 512],
                                  start=True, stop=False)
                        te.matmul(slotA, self.cE0.ap()[:, :],
                                  XCO.ap()[:, hp * 512:(hp + 1) * 512],
                                  start=False, stop=True)
                        te.matmul(slotB, ktile,
                                  QB.ap()[:, (2 * hp + 1) * 512:(2 * hp + 1) * 512 + 512],
                                  start=True, stop=False)
                        self.te.inc(te.matmul(
                            slotB, self.cE64.ap()[:, :],
                            XCO.ap()[:, hp * 512:(hp + 1) * 512],
                            start=False, stop=True), 1)
                        if kc >= 2:
                            emit_av_pair(te, hp, kc - 2, schp, vehp)
                    emit_av_pair(te, hp, 14, schp, vehp)
                    emit_av_pair(te, hp, 15, schp, vehp)
                blk.tensor(te_attn)
                assert self.te.n == tehp + 16 and self.te2.n == avhp + 32

                def sc_attn(sc, hp=hp, tehp=tehp):
                    for kc in range(16):
                        src_ = self.PSA.ap()[:, (kc % 2) * 1024:(kc % 2) * 1024 + 1024]
                        sc.wait_ge(self.te.h, tehp + kc + 1)
                        self.sc.inc(sc.activation(
                            PT.ap()[:, (kc % 3) * 1024:(kc % 3) * 1024 + 1024],
                            src_, AF.Exp, bias=0.0, scale=0.125), 1)
                blk.scalar(sc_attn)
                assert self.sc.n == schp + 16

                pend[0] = (hp, vehp)

                def ve_norm(ve, hp=hp, avhp=avhp):
                    pr = (hp % 2) * 1024
                    ve.wait_ge(self.te2.h, avhp + 32)
                    with self.nc.allow_low_precision(reason="softmax recip rounds to bf16 like any bf16 operand"):
                        self.ve.inc(ve.reciprocal(
                            self.RC.ap()[0:1, 0:1024],
                            self.PSB.ap()[64:65, pr:pr + 1024]), 1)
                    self.ve.inc(ve.engine_nop(), 1)
                    ve.wait_ge(self.sc2.h, s2q + hp + 1)
                    self.ve.inc(ve.tensor_copy(
                        self.RBs.ap()[0:64, :],
                        self.PSB.ap()[64:128, pr:pr + 512]), 1)
                    self.ve.inc(ve.tensor_copy(
                        self.RBs.ap()[64:128, :],
                        self.PSB.ap()[64:128, pr + 512:pr + 1024]), 1)
                    ve.drain()
                    self.ve.inc(ve.tensor_mul(
                        XCO.ap()[0:64, hp * 512:(hp + 1) * 512],
                        self.PSB.ap()[0:64, pr:pr + 512], self.RBs.ap()[0:64, :]), 1)
                    self.ve.inc(ve.tensor_mul(
                        XCO.ap()[64:128, hp * 512:(hp + 1) * 512],
                        self.PSB.ap()[0:64, pr + 512:pr + 1024],
                        self.RBs.ap()[64:128, :]), 1)
                blk.vector(ve_norm)
                assert self.ve.n == vehp + 6

            def te_bc7(te):
                emit_bc(te, *pend[0])
            blk.tensor(te_bc7)
            pend[0] = None

            d1 = [wp[0].n, wp[1].n]
            t1 = self.te.n
            v1 = self.ve.n

            def gp_o(gp):
                for ec in range(8):
                    if ec < 2:
                        gp.wait_ge(self.te.h, t0 + 16)
                    else:
                        gp.wait_ge(self.ve.h, v1 + (ec - 1))
                    src = self.woutT[:, ec * 128:(ec + 1) * 128]
                    src = src.rearrange("(fc p) n -> p fc n", p=128)
                    dst = WS2.ap()[:, (ec % 2) * 1024:(ec % 2 + 1) * 1024]
                    dst = dst.rearrange("p (fc n) -> p fc n", n=128)
                    wp[ec % 2].inc(gp.dma_start(dst, src), 16)
            blk.gpsimd(gp_o)

            def te_o(te):
                te.wait_ge(self.ve.h, v1)
                for ec in range(8):
                    te.wait_ge(wp[ec % 2].h, d1[ec % 2] + 16 * (ec // 2 + 1))
                    if ec >= 2:
                        te.wait_ge(self.ve.h, v1 + (ec - 1))
                    for fc in range(8):
                        ins = te.matmul(
                            self.PSA.ap()[:, (ec % 2) * 512:(ec % 2) * 512 + 512],
                            WS2.ap()[:, (ec % 2) * 1024 + fc * 128:(ec % 2) * 1024 + (fc + 1) * 128],
                            XCO.ap()[:, fc * 512:(fc + 1) * 512],
                            start=(fc == 0), stop=(fc == 7))
                        if fc == 7:
                            self.te.inc(ins, 1)
            blk.tensor(te_o)
            assert self.te.n == t1 + 8

            def ve_o(ve):
                for ec in range(8):
                    ve.wait_ge(self.te.h, t1 + ec + 1)
                    self.ve.inc(ve.tensor_add(
                        self.Xtok(ec, q0, q0 + 512),
                        self.PSA.ap()[:, (ec % 2) * 512:(ec % 2) * 512 + 512],
                        self.Xtok(ec, q0, q0 + 512)), 1)
            blk.vector(ve_o)
            assert self.ve.n == v1 + 8
        self.barrier()


    def p3_qkv(self, s, l, QI, KI, VI, WSI):
        nc = self.nc
        blk = self.block
        wp = [self.dwa, self.dwb]
        d0 = [wp[0].n, wp[1].n]
        t0 = self.te.n
        s0 = self.sc.n
        v0 = self.ve.n

        nxc = 32 if l == 1 else 0

        def ve_pre(ve):
            vov = VI.ap().rearrange("p (tc h w) -> p tc h w", h=8, w=65)
            ve.memset(vov[:, :, :, 64:65], 1.0)
            if l == 1:
                for c in range(8):
                    for tw in range(4):
                        self.ve.inc(ve.tensor_copy(
                            self.XCtok(c, tw * 512, (tw + 1) * 512),
                            self.Xtok(c, tw * 512, (tw + 1) * 512)), 1)
        blk.vector(ve_pre)
        assert self.ve.n == v0 + nxc

        def gp_qkv(gp):
            for oc in range(8):
                if oc >= 2:
                    gp.wait_ge(self.sc.h, s0 + 4 * (oc - 1))
                src = self.wqkv_i[l, :, oc * 128:(oc + 1) * 128]
                src = src.rearrange("(ic p) n -> p ic n", p=128)
                dst = WSI.ap()[:, (oc % 2) * 1024:(oc % 2 + 1) * 1024]
                dst = dst.rearrange("p (ic n) -> p ic n", n=128)
                wp[oc % 2].inc(gp.dma_start(dst, src), 16)
        blk.gpsimd(gp_qkv)

        def te_qkv(te):
            te.wait_ge(self.ve.h, v0 + nxc)
            for oc in range(8):
                te.wait_ge(wp[oc % 2].h, d0[oc % 2] + 16 * (oc // 2 + 1))
                ps = self.PSA if oc % 2 == 0 else self.PSB
                if oc >= 2:
                    te.wait_ge(self.sc.h, s0 + 4 * (oc - 1))
                for ic in range(8):
                    for tw in range(4):
                        ins = te.matmul(
                            ps.ap()[:, tw * 512:(tw + 1) * 512],
                            WSI.ap()[:, (oc % 2) * 1024 + ic * 128:(oc % 2) * 1024 + (ic + 1) * 128],
                            self.XCin(ic, tw * 512, (tw + 1) * 512),
                            start=(ic == 0), stop=(ic == 7))
                        if ic == 7:
                            self.te.inc(ins, 1)
        blk.tensor(te_qkv)
        assert self.te.n == t0 + 32

        def sc_qkv(sc):
            for oc in range(8):
                ps = self.PSA if oc % 2 == 0 else self.PSB
                dstb = QI if oc < 4 else KI
                for tw in range(4):
                    sc.wait_ge(self.te.h, t0 + 4 * oc + tw + 1)
                    self.sc.inc(sc.copy(
                        dstb.ap()[:, (oc % 4) * T + tw * 512:(oc % 4) * T + (tw + 1) * 512],
                        ps.ap()[:, tw * 512:(tw + 1) * 512]), 1)
        blk.scalar(sc_qkv)
        assert self.sc.n == s0 + 32

        d1 = self.dma_w.n
        t1 = self.te.n
        s1 = self.sc.n

        def gp_v(gp):
            for vh in range(4):
                if vh == 0:
                    gp.wait_ge(self.sc.h, s0 + 28)
                else:
                    gp.wait_ge(self.sc.h, s1 + 16 * vh)
                src = self.wqkv_i[l, :, 1024 + vh * 128: 1024 + (vh + 1) * 128]
                src = src.rearrange("(ic p) n -> p ic n", p=128)
                dst = WSI.ap()[:, 0:1024].rearrange("p (ic n) -> p ic n", n=128)
                self.dma_w.inc(gp.dma_start(dst, src), 16)
        blk.gpsimd(gp_v)

        def te_v(te):
            te.wait_ge(self.sc.h, s0 + 32)
            for vh in range(4):
                te.wait_ge(self.dma_w.h, d1 + 16 * (vh + 1))
                for tc in range(16):
                    i = vh * 16 + tc
                    if i >= 4:
                        te.wait_ge(self.sc.h, s1 + i - 3)
                    for ic in range(8):
                        ins = te.matmul(
                            self.PSA.ap()[:, (i % 4) * 512:(i % 4) * 512 + 128],
                            self.XCin(ic, tc * 128, (tc + 1) * 128),
                            WSI.ap()[:, ic * 128:(ic + 1) * 128],
                            start=(ic == 0), stop=(ic == 7))
                        if ic == 7:
                            self.te.inc(ins, 1)
        blk.tensor(te_v)
        assert self.te.n == t1 + 64

        def sc_v(sc):
            vov = VI.ap().rearrange("p (tc h w) -> p tc h w", h=8, w=65)
            for vh in range(4):
                for tc in range(16):
                    i = vh * 16 + tc
                    sc.wait_ge(self.te.h, t1 + i + 1)
                    dst = vov[:, tc, vh * 2:(vh + 1) * 2, 0:64]
                    srcp = self.PSA.ap()[:, (i % 4) * 512:(i % 4) * 512 + 128]
                    self.sc.inc(sc.copy(dst, srcp.rearrange("p (h w) -> p h w", w=64)), 1)
        blk.scalar(sc_v)
        assert self.sc.n == s1 + 64
        self.barrier()

    def p3_prep(self, s, l, B, side, TBL, NRM2, ROT, SQI):
        nc = self.nc
        blk = self.block
        scol = (self.cqsc if side == "q" else self.cksc).ap()[:, l:l + 1]
        tbl_src = self.rotq if side == "q" else self.rotk
        dx0 = self.dma_x.n
        t0 = self.te.n
        t2_0 = self.te2.n
        v0 = self.ve.n
        s0 = self.sc.n
        g0 = self.gp.n

        def sy_tbl(sy):
            self.dma_x.inc(sy.dma_start(TBL.ap()[:, :], tbl_src[:, :]), 16)
        blk.sync(sy_tbl)

        with (
            self.sb("MXS", [128, 32], F32) as MXS,
            self.sb("INVS", [128, 32], F32) as INVS,
            self.sb("DST", [128, 1024], BF16) as DST,
        ):
            pos = {}

            def ve_main(ve):
                n = [0]

                def inc(ins, key=None):
                    self.ve.inc(ins, 1)
                    n[0] += 1
                    if key:
                        pos[key] = n[0]

                def front(hp):
                    if hp >= 1:
                        ve.wait_ge(self.te.h, t0 + hp)
                    inc(ve.tensor_mul(
                        SQI.ap()[:, :],
                        B.ap()[:, hp * T:(hp + 1) * T],
                        B.ap()[:, hp * T:(hp + 1) * T]), ("sq", hp))
                    ve.wait_ge(self.te.h, t0 + hp + 1)
                    if hp >= 1:
                        ve.wait_ge(self.sc.h, s0 + hp)
                    inc(ve.tensor_scalar(
                        MXS.ap()[:, :], self.PSB.ap()[:, 0:32],
                        1e-24, None, ALU.max), ("max", hp))
                    ve.drain()
                    with self.nc.allow_low_precision(reason="l2norm recip feeds fp16 operands"):
                        inc(ve.reciprocal(MXS.ap()[:, :], MXS.ap()[:, :]),
                            ("recip", hp))

                def tail(hp):
                    ve.drain()
                    for (dst0, src0, sgn) in ((0, 32, -1.0), (32, 0, 1.0),
                                              (64, 96, -1.0), (96, 64, 1.0)):
                        inc(ve.tensor_scalar(
                            ROT.ap()[dst0:dst0 + 32, :],
                            B.ap()[src0:src0 + 32, hp * T:(hp + 1) * T],
                            sgn, None, ALU.mult))
                    ve.drain()
                    inc(ve.tensor_mul(
                        B.ap()[:, hp * T:(hp + 1) * T],
                        B.ap()[:, hp * T:(hp + 1) * T], TBL.ap()[:, 0:T]))
                    inc(ve.tensor_mul(
                        ROT.ap()[:, :], ROT.ap()[:, :], TBL.ap()[:, T:2 * T]))
                    ve.drain()
                    inc(ve.tensor_add(
                        B.ap()[:, hp * T:(hp + 1) * T],
                        B.ap()[:, hp * T:(hp + 1) * T], ROT.ap()[:, :]),
                        ("tail", hp))

                ve.wait_ge(self.dma_x.h, dx0 + 16)
                for hp in range(4):
                    front(hp)
                    if hp >= 1:
                        tail(hp - 1)
                    ve.wait_ge(self.te2.h, t2_0 + 8 * hp + 8)
                    inc(ve.scalar_tensor_tensor(
                        B.ap()[:, hp * T:(hp + 1) * T],
                        B.ap()[:, hp * T:(hp + 1) * T],
                        scol,
                        self.PSA.ap()[:, 0:2048],
                        ALU.mult, ALU.mult), ("stt", hp))
                tail(3)
            blk.vector(ve_main)
            assert self.ve.n == v0 + 44

            def sc_main(sc):
                for hp in range(4):
                    sc.wait_ge(self.ve.h, v0 + pos[("recip", hp)])
                    if hp >= 1:
                        sc.wait_ge(self.gp.h, g0 + 8 * hp)
                    self.sc.inc(sc.activation(
                        INVS.ap()[:, :], MXS.ap()[:, :], AF.Sqrt), 1)
            blk.scalar(sc_main)
            assert self.sc.n == s0 + 4

            def gp_main(gp):
                for hp in range(4):
                    gp.wait_ge(self.sc.h, s0 + hp + 1)
                    for i in range(8):
                        g = 8 * hp + i
                        if g >= 2:
                            gp.wait_ge(self.te2.h, t2_0 + g - 1)
                        ins = None
                        for c4 in range(4):
                            ins = gp.tensor_scalar(
                                DST.ap()[:, (g % 2) * 512 + c4 * 128:
                                         (g % 2) * 512 + (c4 + 1) * 128],
                                self.cidf32.ap()[:, :],
                                INVS.ap()[:, 4 * i + c4:4 * i + c4 + 1],
                                -1.0, ALU.mult, ALU.mult)
                        self.gp.inc(ins, 1)
            blk.gpsimd(gp_main)
            assert self.gp.n == g0 + 32

            def te_main(te):
                for hp in range(4):
                    te.wait_ge(self.ve.h, v0 + pos[("sq", hp)])
                    i = 0
                    for sd in range(2):
                        for tw in range(4):
                            for c4 in range(4):
                                ins = te.matmul(
                                    self.PSB.ap()[:, i:i + 1],
                                    SQI.ap()[sd * 64:(sd + 1) * 64,
                                             tw * 512 + c4 * 128:tw * 512 + (c4 + 1) * 128],
                                    self.cones2b.ap()[sd * 64:(sd + 1) * 64, sd:sd + 1],
                                    start=True, stop=True,
                                    tile_position=(sd * 64, 0))
                                i += 1
                                if i == 32:
                                    self.te.inc(ins, 1)
                    if hp >= 1:
                        te.wait_ge(self.ve.h, v0 + pos[("stt", hp - 1)])
                    for i in range(8):
                        g = 8 * hp + i
                        sd, tw = divmod(i, 4)
                        te.wait_ge(self.gp.h, g0 + g + 1)
                        self.te2.inc(te.matmul(
                            self.PSA.ap()[sd * 64:(sd + 1) * 64,
                                          tw * 512:(tw + 1) * 512],
                            self.cmones.ap()[:, 0:64],
                            DST.ap()[:, (g % 2) * 512:(g % 2) * 512 + 512],
                            start=True, stop=True,
                            tile_position=(0, sd * 64)), 1)
            blk.tensor(te_main)
            assert self.te.n == t0 + 4 and self.te2.n == t2_0 + 32
        self.barrier()

    def p4_attn(self, s, l, QI, KI, VI, OI, PTI, RCT):
        nc = self.nc
        blk = self.block
        t0 = self.te.n
        t2_0 = self.te2.n
        s0 = self.sc.n
        v0 = self.ve.n

        steps = []
        for qc in range(16):
            kc0 = max(0, qc - 4)
            for hp in range(4):
                steps.append((qc, hp, kc0, qc - kc0 + 1))
        N = len(steps)

        def sreg(i):
            return self.PSA if i % 2 == 0 else self.PSB

        def avreg(i):
            return self.PSA if i % 2 == 0 else self.PSB

        def te_body(te):
            def emit_av(i):
                qc, hp, kc0, nb = steps[i]
                ps = avreg(i)
                if i >= 2:
                    te.wait_ge(self.ve.h, v0 + i - 1)
                te.wait_ge(self.sc.h, s0 + i + 1)
                for side in range(2):
                    for j in range(nb):
                        kc = kc0 + j
                        ins = te.matmul(
                            ps.ap()[:, 1536 + side * 128:1536 + side * 128 + 65],
                            PTI.ap()[:, side * 640 + j * 128: side * 640 + (j + 1) * 128],
                            VI.ap()[:, kc * 520 + (2 * hp + side) * 65: kc * 520 + (2 * hp + side) * 65 + 65],
                            start=(j == 0), stop=(j == nb - 1))
                        if j == nb - 1:
                            self.te2.inc(ins, 1)

            for i, (qc, hp, kc0, nb) in enumerate(steps):
                ps = sreg(i)
                if i >= 2:
                    te.wait_ge(self.sc.h, s0 + (i - 2) + 1)
                for j in range(nb):
                    kc = kc0 + j
                    mask = (self.cmtril if (j == 0 and qc >= 4) else
                            self.cmtriu if j == nb - 1 else None)
                    te.matmul(ps.ap()[:, j * 128:(j + 1) * 128],
                              KI.ap()[0:64, hp * T + kc * 128: hp * T + (kc + 1) * 128],
                              QI.ap()[0:64, hp * T + qc * 128: hp * T + (qc + 1) * 128],
                              start=True, stop=(mask is None), tile_position=(0, 0))
                    if mask is not None:
                        te.matmul(ps.ap()[:, j * 128:(j + 1) * 128],
                                  self.cidf16.ap()[:, :], mask.ap()[:, :],
                                  start=False, stop=True)
                    ins = te.matmul(
                        ps.ap()[:, 640 + j * 128:640 + (j + 1) * 128],
                        KI.ap()[64:128, hp * T + kc * 128: hp * T + (kc + 1) * 128],
                        QI.ap()[64:128, hp * T + qc * 128: hp * T + (qc + 1) * 128],
                        start=True, stop=(mask is None), tile_position=(64, 0))
                    if mask is not None:
                        ins = te.matmul(
                            ps.ap()[:, 640 + j * 128:640 + (j + 1) * 128],
                            self.cidf16.ap()[:, :], mask.ap()[:, :],
                            start=False, stop=True)
                    if j == nb - 1:
                        self.te.inc(ins, 1)
                if i >= 1:
                    emit_av(i - 1)
            emit_av(N - 1)
        blk.tensor(te_body)
        assert self.te.n == t0 + N and self.te2.n == t2_0 + 2 * N

        def ve_body(ve):
            def norm(i):
                qc, hp, kc0, nb = steps[i]
                ps = avreg(i)
                ve.wait_ge(self.te2.h, t2_0 + 2 * i + 2)
                ve.drain()
                with self.nc.allow_low_precision(reason="softmax recip feeds fp16"):
                    ve.reciprocal(RCT.ap()[:, 0:1], ps.ap()[:, 1536 + 64:1536 + 65])
                    ve.reciprocal(RCT.ap()[:, 1:2], ps.ap()[:, 1664 + 64:1664 + 65])
                ve.drain()
                ve.tensor_scalar(
                    OI.ap()[:, qc * 512 + (2 * hp) * 64: qc * 512 + (2 * hp) * 64 + 64],
                    ps.ap()[:, 1536:1536 + 64], RCT.ap()[:, 0:1], None, ALU.mult)
                self.ve.inc(ve.tensor_scalar(
                    OI.ap()[:, qc * 512 + (2 * hp + 1) * 64: qc * 512 + (2 * hp + 1) * 64 + 64],
                    ps.ap()[:, 1664:1664 + 64], RCT.ap()[:, 1:2], None, ALU.mult), 1)

            for i in range(1, N):
                norm(i - 1)
            norm(N - 1)
        blk.vector(ve_body)
        assert self.ve.n == v0 + N

        def sc_body(sc):
            for i, (qc, hp, kc0, nb) in enumerate(steps):
                sc.wait_ge(self.te.h, t0 + i + 1)
                if i >= 1:
                    sc.wait_ge(self.te2.h, t2_0 + 2 * (i - 1) + 2)
                pv = sreg(i).ap()[:, 0:1280].rearrange(
                    "p (s n) -> p s n", n=640)[:, :, 0:nb * 128]
                ov = PTI.ap()[:, 0:1280].rearrange(
                    "p (s n) -> p s n", n=640)[:, :, 0:nb * 128]
                self.sc.inc(sc.activation(
                    ov, pv, AF.Exp, bias=-C_IN, scale=8.0), 1)
        blk.scalar(sc_body)
        assert self.sc.n == s0 + N
        self.barrier()

    def p4_wo(self, s, l, OI, WSI):
        nc = self.nc
        blk = self.block
        d0 = self._wo_d0
        t0 = self.te.n
        t2_0 = self.te2.n
        v0 = self.ve.n

        evac_pos = [1, 2, 11, 20]
        adda_end = [6, 15, 24, 32]
        addb_end = [10, 19, 28, 36]

        def tp(tw):
            off = (tw % 2) * 1024
            return self.PSB.ap()[:, off:off + 1024].bitcast(F16)

        with self.sb("OIF", [128, 2 * 2048], F16) as OIF:
            def te_body(te):
                def T(tw):
                    if tw >= 2:
                        te.wait_ge(self.ve.h, v0 + evac_pos[tw - 2])
                    v_ = tp(tw)
                    for fc in range(4):
                        for tcq in range(4):
                            tc = tw * 4 + tcq
                            ins = te.transpose(
                                v_[:, (fc * 4 + tcq) * 128:(fc * 4 + tcq + 1) * 128],
                                OI.ap()[:, tc * 512 + fc * 128: tc * 512 + (fc + 1) * 128],
                                self.cidf16.ap()[:, :])
                    self.te.inc(ins, 1)

                def MM(tw, eh):
                    par = tw % 2
                    if eh == 0:
                        te.wait_ge(self.ve.h, v0 + evac_pos[tw])
                        if tw >= 1:
                            te.wait_ge(self.ve.h, v0 + addb_end[tw - 1])
                    else:
                        te.wait_ge(self.ve.h, v0 + adda_end[tw])
                    for fc in range(4):
                        for e4 in range(4):
                            ec = eh * 4 + e4
                            ins = te.matmul(
                                self.PSA.ap()[:, e4 * 512:(e4 + 1) * 512],
                                WSI.ap()[:, fc * 1024 + ec * 128: fc * 1024 + (ec + 1) * 128],
                                OIF.ap()[:, par * 2048 + fc * 512: par * 2048 + (fc + 1) * 512],
                                start=(fc == 0), stop=(fc == 3))
                            if fc == 3:
                                self.te2.inc(ins, 1)

                te.wait_ge(self.dma_w.h, d0 + 16)
                T(0)
                T(1)
                for tw in range(4):
                    MM(tw, 0)
                    if tw < 2:
                        T(tw + 2)
                    MM(tw, 1)
            blk.tensor(te_body)
            assert self.te.n == t0 + 4 and self.te2.n == t2_0 + 32

            def ve_body(ve):
                def evac(tw):
                    par = tw % 2
                    ve.wait_ge(self.te.h, t0 + tw + 1)
                    self.ve.inc(ve.tensor_copy(
                        OIF.ap()[:, par * 2048:(par + 1) * 2048], tp(tw)), 1)

                def adds(tw, eh):
                    for e4 in range(4):
                        ec = eh * 4 + e4
                        ve.wait_ge(self.te2.h, t2_0 + 8 * tw + 4 * eh + e4 + 1)
                        self.ve.inc(ve.tensor_add(
                            self.Xtok(ec, tw * 512, (tw + 1) * 512),
                            self.PSA.ap()[:, e4 * 512:(e4 + 1) * 512],
                            self.Xtok(ec, tw * 512, (tw + 1) * 512)), 1)

                evac(0); evac(1); adds(0, 0); adds(0, 1); evac(2)
                adds(1, 0); adds(1, 1); evac(3); adds(2, 0); adds(2, 1)
                adds(3, 0); adds(3, 1)
            blk.vector(ve_body)
            assert self.ve.n == v0 + 36
        self.barrier()

    def emit_inner_ln(self, s, l):
        nc = self.nc
        blk = self.block
        with (
            self.sb("lnsq", [128, 8 * 512], BF16) as sqt,
            self.sb("ftmp", [128, 8 * 512], F32) as ftmp,
            self.sb("MU2", [1, 2048], F32) as MU2,
            self.sb("ISD2", [1, 2048], F32) as ISD2,
        ):
            v0 = self.ve.n
            t0 = self.te.n
            sc0 = self.sc.n
            t2_0 = self.te2.n
            v2_0 = self.ve2.n
            s2_0 = self.sc2.n

            def ve_main(ve):
                ve.wait_ge(self.dma_s.h, self.dma_s.n)
                for tw in range(4):
                    if tw >= 1:
                        ve.wait_ge(self.te.h, t0 + 16 * tw)
                    for c in range(8):
                        self.ve.inc(ve.tensor_copy(
                            self.XCtok(c, tw * 512, (tw + 1) * 512),
                            self.Xtok(c, tw * 512, (tw + 1) * 512)), 1)
                    ve.drain()
                    for c in range(8):
                        self.ve.inc(ve.tensor_mul(
                            sqt.ap()[:, c * 512:(c + 1) * 512],
                            self.XCin(c, tw * 512, (tw + 1) * 512),
                            self.XCin(c, tw * 512, (tw + 1) * 512)), 1)
                ve.wait_ge(self.te.h, t0 + 64)
                mu = MU2.ap()[0:1, 0:2048]
                isd = ISD2.ap()[0:1, 0:2048]
                self.ve.inc(ve.tensor_scalar(
                    mu, self.PSB.ap()[0:1, 0:2048], 1.0 / 1024.0, None, ALU.mult), 1)
                ve.drain()
                self.ve.inc(ve.tensor_mul(isd, mu, mu), 1)
                ve.drain()
                self.ve.inc(ve.scalar_tensor_tensor(
                    isd, self.PSA.ap()[0:1, 0:2048], 1.0 / 1024.0, isd,
                    ALU.mult, ALU.subtract), 1)
                ve.drain()
                self.ve.inc(ve.tensor_scalar(isd, isd, 1e-5, None, ALU.add), 1)
                ve.drain()
                self.ve.inc(ve.reciprocal(isd, isd), 1)
            blk.vector(ve_main)
            assert self.ve.n == v0 + 69

            def te_sums(te):
                for tw in range(4):
                    te.wait_ge(self.ve.h, v0 + 16 * tw + 8)
                    for c in range(8):
                        self.te.inc(te.matmul(
                            self.PSB.ap()[0:1, tw * 512:(tw + 1) * 512],
                            self.cones1h.ap()[:, :],
                            self.XCin(c, tw * 512, (tw + 1) * 512),
                            start=(c == 0), stop=(c == 7)), 1)
                    te.wait_ge(self.ve.h, v0 + 16 * (tw + 1))
                    for c in range(8):
                        self.te.inc(te.matmul(
                            self.PSA.ap()[0:1, tw * 512:(tw + 1) * 512],
                            self.cones1b.ap()[:, :],
                            sqt.ap()[:, c * 512:(c + 1) * 512],
                            start=(c == 0), stop=(c == 7)), 1)
                te.wait_ge(self.sc.h, sc0 + 1)
                for tw in range(4):
                    self.te2.inc(te.matmul(
                        self.PSB.ap()[:, tw * 512:(tw + 1) * 512],
                        self.cone1f.ap()[:, :],
                        MU2.ap()[0:1, tw * 512:(tw + 1) * 512],
                        start=True, stop=True), 1)
                    self.te2.inc(te.matmul(
                        self.PSA.ap()[:, tw * 512:(tw + 1) * 512],
                        self.cone1f.ap()[:, :],
                        ISD2.ap()[0:1, tw * 512:(tw + 1) * 512],
                        start=True, stop=True), 1)
            blk.tensor(te_sums)
            assert self.te.n == t0 + 64 and self.te2.n == t2_0 + 8

            def sc_rows(sc):
                sc.wait_ge(self.ve.h, v0 + 69)
                self.sc.inc(sc.activation(
                    ISD2.ap()[0:1, 0:2048],
                    ISD2.ap()[0:1, 0:2048], AF.Sqrt), 1)
            blk.scalar(sc_rows)
            assert self.sc.n == sc0 + 1

            def ve_apply(ve):
                ve.wait_ge(self.te.h, t0 + 64)
                for tw in range(4):
                    ve.wait_ge(self.te2.h, t2_0 + 2 * (tw + 1))
                    mub = self.PSB.ap()[:, tw * 512:(tw + 1) * 512]
                    isb = self.PSA.ap()[:, tw * 512:(tw + 1) * 512]
                    for c in range(8):
                        slot = ftmp.ap()[:, c * 512:c * 512 + 512]
                        if tw >= 1:
                            ve.wait_ge(self.sc2.h, s2_0 + 8 * (tw - 1) + c + 1)
                        self.ve2.inc(ve.tensor_sub(
                            slot, self.Xin(c, tw * 512, (tw + 1) * 512), mub), 1)
                        ve.drain()
                        self.ve2.inc(ve.tensor_mul(slot, slot, isb), 1)
            blk.vector(ve_apply)
            assert self.ve2.n == v2_0 + 64

            def sc_apply(sc):
                for tw in range(4):
                    for c in range(8):
                        sc.wait_ge(self.ve2.h, v2_0 + 16 * tw + 2 * c + 2)
                        self.sc2.inc(sc.activation(
                            self.XCtok(c, tw * 512, (tw + 1) * 512),
                            ftmp.ap()[:, c * 512:c * 512 + 512],
                            AF.Identity,
                            bias=self.clnb.ap()[:, l * 8 + c:l * 8 + c + 1],
                            scale=self.clng.ap()[:, l * 8 + c:l * 8 + c + 1]), 1)
            blk.scalar(sc_apply)
            assert self.sc2.n == s2_0 + 32

    def p5_mlp(self, s, l):
        nc = self.nc
        blk = self.block
        self.emit_inner_ln(s, l)

        def ve_b2(ve):
            for c in range(8):
                self.ve.inc(ve.tensor_scalar(
                    self.Xtok(c, 0, T), self.Xtok(c, 0, T),
                    self.cb2.ap()[:, l * 8 + c:l * 8 + c + 1], None, ALU.add), 1)
        blk.vector(ve_b2)
        self.barrier()

        with (
            self.sb("HBUF", [128, NOC1 * 1024], F16) as HBUF,
            self.sb("WS3", [128, 8192], F16) as WS3,
            self.sb("GT", [128, 2048], F16) as GT,
            self.sb("AT", [128, 2048], F16) as AT,
        ):
            wp = [self.dwa, self.dwb]
            for th in range(2):
                self.p5_th(s, l, th, HBUF, WS3, GT, AT, wp)

    def p5_th(self, s, l, th, HBUF, WS3, GT, AT, wp):
        nc = self.nc
        blk = self.block
        tb = th * 1024
        d0 = [wp[0].n, wp[1].n]
        t0 = self.te.n
        s0 = self.sc.n
        v0 = self.ve.n

        def gp_w1(gp):
            for oc in range(NOC1):
                w = min(128, HID - oc * 128)
                if oc >= 2:
                    gp.wait_ge(self.sc.h, s0 + 4 * (oc - 1))
                srca = self.w1_i[l, :, oc * 128: oc * 128 + w]
                srca = srca.rearrange("(ic p) n -> p ic n", p=128)
                dsta = WS3.ap()[:, (oc % 2) * 2048:(oc % 2) * 2048 + 8 * w]
                dsta = dsta.rearrange("p (ic n) -> p ic n", n=w)
                wp[oc % 2].inc(gp.dma_start(dsta, srca), 16)
                srcg = self.w1_i[l, :, HID + oc * 128: HID + oc * 128 + w]
                srcg = srcg.rearrange("(ic p) n -> p ic n", p=128)
                dstg = WS3.ap()[:, 4096 + (oc % 2) * 2048:4096 + (oc % 2) * 2048 + 8 * w]
                dstg = dstg.rearrange("p (ic n) -> p ic n", n=w)
                wp[oc % 2].inc(gp.dma_start(dstg, srcg), 16)
        blk.gpsimd(gp_w1)

        def te_w1(te):
            for oc in range(NOC1):
                w = min(128, HID - oc * 128)
                te.wait_ge(wp[oc % 2].h, d0[oc % 2] + 32 * (oc // 2 + 1))
                ps = self.PSA if oc % 2 == 0 else self.PSB
                if oc >= 2:
                    te.wait_ge(self.sc.h, s0 + 4 * (oc - 1))
                for ic in range(8):
                    for tw in range(2):
                        insa = te.matmul(
                            ps.ap()[0:w, tw * 512:(tw + 1) * 512],
                            WS3.ap()[:, (oc % 2) * 2048 + ic * w:(oc % 2) * 2048 + (ic + 1) * w],
                            self.XCtok(ic, tb + tw * 512, tb + (tw + 1) * 512),
                            start=(ic == 0), stop=(ic == 7))
                        insg = te.matmul(
                            ps.ap()[0:w, 1024 + tw * 512:1024 + (tw + 1) * 512],
                            WS3.ap()[:, 4096 + (oc % 2) * 2048 + ic * w:4096 + (oc % 2) * 2048 + (ic + 1) * w],
                            self.XCtok(ic, tb + tw * 512, tb + (tw + 1) * 512),
                            start=(ic == 0), stop=(ic == 7))
                        if ic == 7:
                            self.te.inc(insa, 1)
                            self.te.inc(insg, 1)
        blk.tensor(te_w1)
        assert self.te.n == t0 + 4 * NOC1

        def sc_w1(sc):
            for oc in range(NOC1):
                w = min(128, HID - oc * 128)
                ps = self.PSA if oc % 2 == 0 else self.PSB
                if oc >= 2:
                    sc.wait_ge(self.ve.h, v0 + 2 * (oc - 1))
                for tw in range(2):
                    sc.wait_ge(self.te.h, t0 + 4 * oc + 2 * tw + 2)
                    sc.activation(
                        GT.ap()[0:w, (oc % 2) * 1024 + tw * 512:(oc % 2) * 1024 + (tw + 1) * 512],
                        ps.ap()[0:w, 1024 + tw * 512:1024 + (tw + 1) * 512],
                        AF.Gelu,
                        bias=self.cb1g.ap()[0:w, l * NOC1 + oc:l * NOC1 + oc + 1])
                    self.sc.inc(sc.activation(
                        AT.ap()[0:w, (oc % 2) * 1024 + tw * 512:(oc % 2) * 1024 + (tw + 1) * 512],
                        ps.ap()[0:w, tw * 512:(tw + 1) * 512],
                        AF.Identity,
                        bias=self.cb1a.ap()[0:w, l * NOC1 + oc:l * NOC1 + oc + 1]), 2)
        blk.scalar(sc_w1)
        assert self.sc.n == s0 + 4 * NOC1

        def ve_w1(ve):
            for oc in range(NOC1):
                w = min(128, HID - oc * 128)
                ve.wait_ge(self.sc.h, s0 + 4 * oc + 4)
                for tw in range(2):
                    self.ve.inc(ve.tensor_mul(
                        HBUF.ap()[0:w, oc * 1024 + tw * 512:oc * 1024 + (tw + 1) * 512],
                        AT.ap()[0:w, (oc % 2) * 1024 + tw * 512:(oc % 2) * 1024 + (tw + 1) * 512],
                        GT.ap()[0:w, (oc % 2) * 1024 + tw * 512:(oc % 2) * 1024 + (tw + 1) * 512]), 1)
        blk.vector(ve_w1)
        assert self.ve.n == v0 + 2 * NOC1

        grps = []
        for g in range(cdiv(NOC1, 4)):
            hc0 = g * 4
            grps.append((hc0, min(4, NOC1 - hc0)))
        steps = [(oh, gi) for oh in range(2) for gi in range(len(grps))]
        ndma = []
        for (oh, gi) in steps:
            hc0, ng = grps[gi]
            rows = min(512, HID - hc0 * 128)
            ndma.append(1 if rows % 128 == 0 else 2)
        par_tot = [0, 0]
        dma_tgt = []
        for i, n in enumerate(ndma):
            par_tot[i % 2] += 16 * n
            dma_tgt.append(par_tot[i % 2])
        d1 = [wp[0].n, wp[1].n]
        t1 = self.te.n
        v1 = self.ve.n

        def gp_w2(gp):
            for i, (oh, gi) in enumerate(steps):
                hc0, ng = grps[gi]
                if i < 2:
                    gp.wait_ge(self.ve.h, v1)
                else:
                    gp.wait_ge(self.te.h, t1 + i - 1)
                r0 = hc0 * 128
                rows = min(512, HID - r0)
                full = (rows // 128) * 128
                src = self.w2_i[l, r0:r0 + full, oh * 512:(oh + 1) * 512]
                src = src.rearrange("(g p) n -> p g n", p=128)
                dst = WS3.ap()[:, (i % 2) * 4096:(i % 2) * 4096 + (full // 128) * 512]
                dst = dst.rearrange("p (g n) -> p g n", n=512)
                ins = gp.dma_start(dst, src)
                if rows > full:
                    wp[i % 2].inc(ins, 16)
                    src2 = self.w2_i[l, r0 + full:r0 + rows, oh * 512:(oh + 1) * 512]
                    dst2 = WS3.ap()[0:rows - full,
                                    (i % 2) * 4096 + (full // 128) * 512:(i % 2) * 4096 + (full // 128) * 512 + 512]
                    wp[i % 2].inc(gp.dma_start(dst2, src2), 16)
                else:
                    wp[i % 2].inc(ins, 16)
        blk.gpsimd(gp_w2)

        def te_w2(te):
            te.wait_ge(self.ve.h, v1)
            te.wait_ge(self.sc.h, s0 + 4 * NOC1)
            for i, (oh, gi) in enumerate(steps):
                hc0, ng = grps[gi]
                te.wait_ge(wp[i % 2].h, d1[i % 2] + dma_tgt[i])
                if oh == 1 and gi == 0:
                    te.wait_ge(self.ve.h, v1 + 8)
                for g in range(ng):
                    hc = hc0 + g
                    w = min(128, HID - hc * 128)
                    for ofc in range(4):
                        for tw in range(2):
                            ps = self.PSA if ofc < 2 else self.PSB
                            bank = (ofc % 2) * 1024 + tw * 512
                            ins = te.matmul(
                                ps.ap()[:, bank:bank + 512],
                                WS3.ap()[0:w, (i % 2) * 4096 + g * 512 + ofc * 128:(i % 2) * 4096 + g * 512 + (ofc + 1) * 128],
                                HBUF.ap()[0:w, hc * 1024 + tw * 512: hc * 1024 + (tw + 1) * 512],
                                start=(hc == 0), stop=(hc == NOC1 - 1))
                            if ofc == 3 and tw == 1 and g == ng - 1:
                                self.te.inc(ins, 1)
        blk.tensor(te_w2)
        assert self.te.n == t1 + len(steps)

        def ve_w2(ve):
            for oh in range(2):
                ve.wait_ge(self.te.h, t1 + len(grps) * (oh + 1))
                for ofc in range(4):
                    for tw in range(2):
                        ps = self.PSA if ofc < 2 else self.PSB
                        bank = (ofc % 2) * 1024 + tw * 512
                        c = oh * 4 + ofc
                        self.ve.inc(ve.tensor_add(
                            self.Xtok(c, tb + tw * 512, tb + (tw + 1) * 512),
                            ps.ap()[:, bank:bank + 512],
                            self.Xtok(c, tb + tw * 512, tb + (tw + 1) * 512)), 1)
        blk.vector(ve_w2)
        assert self.ve.n == v1 + 16
        self.barrier()


    def p345_attn(self, s, l):
        with (
            self.sb("OI", [128, 16 * 512], F16) as OI,
            self.sb("WSI2", [128, 4 * 1024], F16) as WSI2,
        ):
            with (
                self.sb("QI", [128, 4 * T], F16) as QI,
                self.sb("KI", [128, 4 * T], F16) as KI,
                self.sb("VI", [128, 16 * 520], BF16) as VI,
            ):
                with self.sb("WSI", [128, 2048], F16) as WSI:
                    self.p3_qkv(s, l, QI, KI, VI, WSI)
                with (
                    self.sb("TBL", [128, 2 * T], F32) as TBL,
                    self.sb("ROT", [128, T], F16) as ROT,
                    self.sb("SQI", [128, 2048], BF16) as SQI,
                ):
                    self.p3_prep(s, l, QI, "q", TBL, None, ROT, SQI)
                    self.p3_prep(s, l, KI, "k", TBL, None, ROT, SQI)
                self._wo_d0 = self.dma_w.n

                def gp_wo(gp):
                    src_ = self.wo_i[l, :, :].rearrange("(fc p) n -> p fc n", p=128)
                    self.dma_w.inc(gp.dma_start(
                        WSI2.ap().rearrange("p (fc n) -> p fc n", n=1024), src_), 16)
                self.block.gpsimd(gp_wo)
                with (
                    self.sb("PTI", [128, 2 * 640], BF16) as PTI,
                    self.sb("RCT", [128, 2], F32) as RCT,
                ):
                    self.p4_attn(s, l, QI, KI, VI, OI, PTI, RCT)
            self.p4_wo(s, l, OI, WSI2)

    def tap(self, name, s):
        if name not in self.taps:
            return
        blk = self.block
        ap = self.taps[name]

        def sync_body(sy):
            self.dma_o.inc(sy.dma_start(ap[s, :, :], self.X.ap()[:, :]), 16)
            sy.wait_ge(self.dma_o.h, self.dma_o.n)
        blk.sync(sync_body)
        self.barrier()

    def load_consts(self):
        blk = self.block

        def sync_body(sy):
            pairs = [
                (self.cidf32, self.idf32), (self.cmones, self.monesb),
                (self.cones1h, self.ones1h), (self.cones1b, self.ones1b),
                (self.cones2b, self.ones2b), (self.cone1f, self.one1f),
                (self.cone1b, self.one1b),
                (self.cbc2f, self.bc2f), (self.clng, self.lng),
                (self.clnb, self.lnb), (self.cb2, self.b2c),
                (self.cb1a, self.b1a), (self.cb1g, self.b1g),
                (self.cqsc, self.qscc), (self.cksc, self.kscc),
                (self.cm49, self.m49), (self.cmtril, self.mtril),
                (self.cmtriu, self.mtriu), (self.cidf16, self.idf16),
                (self.cE0, self.e0m), (self.cE64, self.e64m),
            ]
            for sb, dr in pairs:
                self.dma_x.inc(sy.dma_start(sb.ap()[:, :], dr[:, :]), 16)
            sy.wait_ge(self.dma_x.h, self.dma_x.n)
        blk.sync(sync_body)
        self.barrier()

    def emit(self):
        nc = self.nc
        from contextlib import ExitStack
        with ExitStack() as stack:
            self.alloc_persistent(stack)
            with nc.Block() as block:
                self.block = block
                self.load_consts()
                for s in range(NSEQ):
                    if self.ph_on("P0"):
                        self.p0(s)
                    self.tap("p0", s)
                    if self.ph_on("P12"):
                        self.p12(s)
                    if self.ph_on("P12"):
                        self.emit_outer_ln(s, make_xc=True, post_copies=True)
                        self.emit_x1_stash(s)
                    self.tap("x1", s)
                    for l in range(2):
                        if self.ph_on(f"L{l}A"):
                            self.p345_attn(s, l)
                        self.tap(f"mid{l}", s)
                        if self.ph_on(f"L{l}M"):
                            self.p5_mlp(s, l)
                        self.tap(f"u{l}", s)
                    if self.ph_on("P6"):
                        self.p6(s, full=self.ph_on("P6full"))



def _rotary_tables():
    half = np.arange(0, 64, 2, dtype=np.float64)
    inv_freq = 1.0 / 10000.0 ** (half / 64)
    t = np.arange(T, dtype=np.float64)
    freqs = np.concatenate([t[:, None] * inv_freq[None, :]] * 2, axis=-1)
    base = (half + 0.4 * 64) / (1.4 * 64)
    power = (t - T // 2) / 256.0
    scale = np.concatenate([base[None, :] ** power[:, None]] * 2, axis=-1)
    cos, sin = np.cos(freqs), np.sin(freqs)
    return (cos * scale, sin * scale, cos / scale, sin / scale)


def _stage_consts(inputs):
    d = {}
    d["idf32"] = np.eye(128, dtype=np.float32)
    d["monesb"] = np.full((128, 128), -1.0, dtype=NPBF16)
    d["ones1h"] = np.ones((128, 1), dtype=NPF16)
    d["ones1b"] = np.ones((128, 1), dtype=NPBF16)
    o2 = np.zeros((128, 2), dtype=NPBF16)
    o2[0:64, 0] = 1.0
    o2[64:128, 1] = 1.0
    d["ones2b"] = o2
    d["one1f"] = np.ones((1, 128), dtype=np.float32)
    d["m49"] = np.full((128, 1), -49.0, dtype=np.float32)
    iidx = np.arange(128)[:, None]; jidx = np.arange(128)[None, :]
    d["mtril"] = np.where(iidx >= jidx, 0.0, -60000.0).astype(NPF16)
    d["mtriu"] = np.where(iidx <= jidx, 0.0, -60000.0).astype(NPF16)
    d["idf16"] = np.eye(128, dtype=NPF16)
    e0 = np.zeros((128, 128), dtype=NPF16); e0[0, :] = -1.0
    e64 = np.zeros((128, 128), dtype=NPF16); e64[64, :] = -1.0
    d["e0m"] = e0
    d["e64m"] = e64
    d["one1b"] = np.ones((1, 128), dtype=NPBF16)
    bc2 = np.zeros((2, 128), dtype=np.float32)
    bc2[0, 0:64] = 1.0
    bc2[1, 64:128] = 1.0
    d["bc2f"] = bc2
    lng = np.zeros((128, 16), np.float32)
    lnb = np.zeros((128, 16), np.float32)
    b2c = np.zeros((128, 16), np.float32)
    for l in range(2):
        for c in range(8):
            lng[:, l * 8 + c] = inputs["L0_ffn_ln_g"][l, c * 128:(c + 1) * 128]
            lnb[:, l * 8 + c] = inputs["L0_ffn_ln_b"][l, c * 128:(c + 1) * 128]
            b2c[:, l * 8 + c] = inputs["L0_ffn_b2"][l, c * 128:(c + 1) * 128]
    d["lng"], d["lnb"], d["b2c"] = lng, lnb, b2c
    b1a = np.zeros((128, 2 * NOC1), np.float32)
    b1g = np.zeros((128, 2 * NOC1), np.float32)
    b1 = inputs["L0_ffn_b1"]
    for l in range(2):
        for oc in range(NOC1):
            w = min(128, HID - oc * 128)
            b1a[0:w, l * NOC1 + oc] = b1[l, oc * 128: oc * 128 + w]
            b1g[0:w, l * NOC1 + oc] = b1[l, HID + oc * 128: HID + oc * 128 + w]
    d["b1a"], d["b1g"] = b1a, b1g
    qsc = np.zeros((128, 2), np.float32)
    ksc = np.zeros((128, 2), np.float32)
    for l in range(2):
        qsc[:, l] = np.tile(inputs["L0_ffn_q_scale"][l], 2)
        ksc[:, l] = np.tile(inputs["L0_ffn_k_scale"][l], 2)
    d["qscc"], d["kscc"] = qsc, ksc
    aq, bq, ak, bk = _rotary_tables()
    rq = np.zeros((128, 2 * T), np.float64)
    rk = np.zeros((128, 2 * T), np.float64)
    rq[:, 0:T] = np.tile(aq.T, (2, 1))
    rq[:, T:2 * T] = np.tile(bq.T, (2, 1))
    rk[:, 0:T] = np.tile(ak.T, (2, 1))
    rk[:, T:2 * T] = np.tile(bk.T, (2, 1))
    d["rotq"] = rq.astype(np.float32)
    d["rotk"] = rk.astype(np.float32)
    return d


def build_nc():
    nc = bass.Bass("TRN2", target_bir_lowering=False)
    b = Build(nc)
    b.declare_io()
    b.emit()
    return nc


def kernel(**inputs):
    inputs = {k: np.ascontiguousarray(np.asarray(v)) for k, v in inputs.items()}
    consts = _stage_consts(inputs)
    wqkvT = np.ascontiguousarray(inputs["L0_attn_w_qkv"].T)
    woutT = np.ascontiguousarray(inputs["L0_attn_w_out"].T)
    nc = build_nc()
    in_maps = []
    for core in range(8):
        m = dict(consts)
        m["xin"] = np.ascontiguousarray(inputs["L0_attn_x"][core * 2:(core + 1) * 2])
        m["wqkvT"] = wqkvT
        m["woutT"] = woutT
        m["wqkv_i"] = inputs["L0_ffn_wqkv"]
        m["wo_i"] = inputs["L0_ffn_wo"]
        m["w1_i"] = inputs["L0_ffn_w1"]
        m["w2_i"] = inputs["L0_ffn_w2"]
        in_maps.append(m)
    res = run_bass_kernel_spmd(nc, in_maps, core_ids=list(range(8)))
    out = np.concatenate([r["out"] for r in res.results], axis=0)
    return out.astype(np.float32)
